# version 10
# baseline (speedup 1.0000x reference)
import numpy as np
import ml_dtypes
import concourse.bass as bass
import concourse.mybir as mybir
from concourse.bass_utils import run_bass_kernel_spmd

F32 = mybir.dt.float32
BF16 = mybir.dt.bfloat16
AF = mybir.ActivationFunctionType
ALU = mybir.AluOpType
AX = mybir.AxisListType

S = 4096
D = 2048
NH = 8
NKV = 4
HD = 256
ROT = 128
WIN = 1024
HFF = 5632
CAP = 50.0
SCALE = HD ** (-0.5)
EPS_LN = 1e-5
EPS_RMS = 1e-6

NCHUNK = S // 128
NG = 4
GC = 8
GQ = GC * 128
DC = D // 128
HC = HFF // 128

ENGINES = ("sync", "gpsimd", "tensor", "vector", "scalar")


class Prog:

    def __init__(self, nc):
        self.nc = nc
        self.q = {e: [] for e in ENGINES}
        self.sems = {}
        self.cnt = {}
        self.last_wait = {}
        self.bar_round = 0
        self.dma_engs = {}

    def sem(self, name):
        if name not in self.sems:
            self.sems[name] = self.nc.alloc_semaphore(name)
            self.cnt[name] = 0
        return name

    def emit(self, eng, builder, waits=(), inc=None):
        ws = []
        for s, v in waits:
            if v <= 0:
                continue
            self.sem(s)
            key = (eng, s)
            if v > self.last_wait.get(key, 0):
                self.last_wait[key] = v
                ws.append((s, v))
        if inc is not None:
            self.sem(inc[0])
            self.cnt[inc[0]] += inc[1]
        self.q[eng].append([builder, ws, inc])
        return self.cnt[inc[0]] if inc is not None else None

    def mark_last(self, eng, sem, n=1):
        entry = self.q[eng][-1]
        assert entry[2] is None, f"last op on {eng} already carries an inc"
        self.sem(sem)
        self.cnt[sem] += n
        entry[2] = (sem, n)
        return self.cnt[sem]

    def c(self, s):
        return self.cnt.get(s, 0)

    def barrier(self):
        self.bar_round += 1
        tgt = 5 * self.bar_round
        for stream, e in self.dma_engs.items():
            self.emit(e, lambda h: h.nop(), waits=[(stream, self.c(stream))])
        for e in ENGINES:
            self.emit(e, lambda h: h.drain(), inc=("bar", 1))
        for e in ENGINES:
            self.emit(e, lambda h: h.nop(), waits=[("bar", tgt)])

    def replay(self, eng, handle):
        for builder, ws, inc in self.q[eng]:
            for s, v in ws:
                handle.wait_ge(self.sems[s], v)
            ins = builder(handle)
            if inc is not None:
                ins.then_inc(self.sems[inc[0]], inc[1])


class Arena:
    def __init__(self, nc, name, nbytes):
        self.t = nc.alloc_sbuf_tensor(name, [128, nbytes], mybir.dt.uint8)
        self.nbytes = nbytes
        self.off = 0
        self.hwm = 0

    def take(self, shape, dt):
        assert shape[0] == 128
        esz = mybir.dt.size(dt)
        n = int(np.prod(shape[1:])) * esz
        off = (self.off + 63) & ~63
        assert off + n <= self.nbytes, f"arena overflow: {off}+{n} > {self.nbytes}"
        self.off = off + n
        self.hwm = max(self.hwm, self.off)
        v = self.t[:, off:off + n].bitcast(dt)
        if len(shape) == 2:
            return v
        names = " ".join(f"a{i}" for i in range(1, len(shape)))
        kw = {f"a{i}": shape[i] for i in range(1, len(shape))}
        return v.rearrange(f"p ({names}) -> p {names}", **kw)

    def reset(self, off=0):
        self.off = off


def ceil_div(a, b):
    return (a + b - 1) // b


def build_nc(dump=False, phases=("p1a", "p1b", "p2", "p3", "p4")):
    nc = bass.Bass("TRN2", target_bir_lowering=False)

    x_d = nc.dram_tensor("x", [S, D], F32, kind="ExternalInput").ap()
    wq_d = nc.dram_tensor("wq", [D, NH * HD], F32, kind="ExternalInput").ap()
    wk_d = nc.dram_tensor("wk", [D, NKV * HD], F32, kind="ExternalInput").ap()
    wv_d = nc.dram_tensor("wv", [D, NKV * HD], F32, kind="ExternalInput").ap()
    wo_d = nc.dram_tensor("wo", [NH * HD, D], F32, kind="ExternalInput").ap()
    qs_d = nc.dram_tensor("qsc", [1, HD], F32, kind="ExternalInput").ap()
    ks_d = nc.dram_tensor("ksc", [1, HD], F32, kind="ExternalInput").ap()
    w1_d = nc.dram_tensor("w1", [HFF, D], F32, kind="ExternalInput").ap()
    w2_d = nc.dram_tensor("w2", [HFF, D], F32, kind="ExternalInput").ap()
    w3_d = nc.dram_tensor("w3", [D, HFF], F32, kind="ExternalInput").ap()
    rope_d = nc.dram_tensor("ropetab", [NCHUNK, 128, 2 * ROT], F32, kind="ExternalInput").ap()
    mask_d = nc.dram_tensor("maskbf", [128, 256], BF16, kind="ExternalInput").ap()
    out_d = nc.dram_tensor("out", [S, D], F32, kind="ExternalOutput").ap()

    skind = "ExternalOutput" if dump else "Internal"
    xn_s = nc.dram_tensor("xn_s", [S, D], BF16, kind=skind).ap()
    q_s = nc.dram_tensor("q_s", [S, NH * HD], BF16, kind=skind).ap()
    k_s = nc.dram_tensor("k_s", [S, NKV * HD], BF16, kind=skind).ap()
    v_s = nc.dram_tensor("v_s", [S, NKV * HD], BF16, kind=skind).ap()
    s_s = nc.dram_tensor("s_s", [S, D], F32, kind=skind).ap()
    hid_s = nc.dram_tensor("hid_s", [HFF, S], BF16, kind=skind).ap()
    w1q_s = nc.dram_tensor("w1q_s", [HFF, D], BF16, kind=skind).ap()
    w2q_s = nc.dram_tensor("w2q_s", [HFF, D], BF16, kind=skind).ap()
    w3q_s = nc.dram_tensor("w3q_s", [D, HFF], BF16, kind=skind).ap()
    dbg_d = nc.dram_tensor("dbg", [128, 6144], F32, kind="ExternalOutput" if dump else "Internal").ap()

    x_v = x_d.rearrange("(c p) d -> p c d", p=128)
    out_v = out_d.rearrange("(c p) d -> p c d", p=128)
    xn_v = xn_s.rearrange("(c p) d -> p c d", p=128)
    q_v = q_s.rearrange("(c p) d -> p c d", p=128)
    k_v = k_s.rearrange("(c p) d -> p c d", p=128)
    v_v = v_s.rearrange("(c p) d -> p c d", p=128)
    s_v = s_s.rearrange("(c p) d -> p c d", p=128)
    hid_v = hid_s.rearrange("(c p) t -> p c t", p=128)
    w1q_v = w1q_s.rearrange("(c p) d -> p c d", p=128)
    w2q_v = w2q_s.rearrange("(c p) d -> p c d", p=128)
    w3q_v = w3q_s.rearrange("(c p) h -> p c h", p=128)

    pr = Prog(nc)

    PS = [nc.alloc_psum_tensor(f"ps{i}", [128, 512], F32) for i in range(8)]
    ps_free = [("mm", 0)] * 8

    mask_sb = nc.alloc_sbuf_tensor("mask_sb", [128, 256], BF16)
    qs_bc = nc.alloc_sbuf_tensor("qs_bc", [128, HD], F32)
    ks_bc = nc.alloc_sbuf_tensor("ks_bc", [128, HD], F32)
    qs_row = nc.alloc_sbuf_tensor("qs_row", [1, HD], F32)
    ks_row = nc.alloc_sbuf_tensor("ks_row", [1, HD], F32)
    ones_fcol = nc.alloc_sbuf_tensor("ones_fcol", [128, 1], F32)
    ones_frow = nc.alloc_sbuf_tensor("ones_frow", [1, 128], F32)
    ones_bcol = nc.alloc_sbuf_tensor("ones_bcol", [128, 1], BF16)
    thr_sb = nc.alloc_sbuf_tensor("thr_sb", [128, 6], F32)
    trash = nc.alloc_sbuf_tensor("trash_sb", [128, D], F32)
    spacer_sb = nc.alloc_sbuf_tensor("spacer_sb", [128, 1], F32)

    AR = Arena(nc, "arena", 194 * 1024)

    def dma(eng, stream, out, in_, waits=(), transpose=False):
        s = pr.sem(stream)
        pr.dma_engs[stream] = eng
        v = pr.emit(eng,
                    lambda h, o=out, i=in_, t=transpose: h.dma_start(out=o, in_=i, transpose=t),
                    waits=waits, inc=(s, 16))
        return v

    def mm(out, lhsT, rhs, start, stop, waits=()):
        pr.emit("tensor",
                lambda h, o=out, l=lhsT, r=rhs, st=start, sp=stop:
                h.matmul(o, l, r, start=st, stop=sp),
                waits=waits)

    def mm_end():
        return pr.mark_last("tensor", "mm")

    def vop(fn, waits=(), inc=None):
        return pr.emit("vector", fn, waits=waits, inc=inc)

    def sop(fn, waits=(), inc=None):
        return pr.emit("scalar", fn, waits=waits, inc=inc)

    def vmark():
        return pr.mark_last("vector", "vec")

    def vspace():
        pr.emit("vector", lambda h: h.tensor_copy(spacer_sb[:], ones_fcol[:]))

    def smark():
        return pr.mark_last("scalar", "sc")

    def psw(banks):
        return [ps_free[b] for b in banks]

    def setup(level=4):
        for t, val in ((ones_fcol, 1.0), (ones_frow, 1.0), (ones_bcol, 1.0)):
            pr.emit("vector", lambda h, a=t, v=val: h.memset(a[:], v))
        c_ones = vmark()
        if level < 1:
            return
        dma("sync", "c0", mask_sb[:], mask_d)
        vq = dma("sync", "c0", qs_row[:], qs_d)
        vk = dma("sync", "c0", ks_row[:], ks_d)
        if level < 2:
            return
        mm(PS[0][:, 0:HD], ones_frow[:], qs_row[:], True, True,
           waits=[("c0", vq), ("vec", c_ones)])
        mm(PS[1][:, 0:HD], ones_frow[:], ks_row[:], True, True,
           waits=[("c0", vk)])
        gid = mm_end()
        if level < 3:
            return
        if level == 9:
            vop(lambda h: h.nop(), waits=[("mm", gid)])
            return
        vop(lambda h: h.tensor_scalar(qs_bc[:], PS[0][:, 0:HD], float(SCALE), None, ALU.mult),
            waits=[("mm", gid)])
        if level < 4:
            cv = vmark()
            ps_free[0] = ("vec", cv)
            return
        vop(lambda h: h.tensor_copy(ks_bc[:], PS[1][:, 0:HD]))
        cv = vmark()
        ps_free[0] = ("vec", cv)
        ps_free[1] = ("vec", cv)

    p0_tile = [AR.take([128, 1024], F32) for _ in range(4)]
    p0_wq = [AR.take([128, 1024], BF16) for _ in range(2)]
    p0_abs = AR.take([128, 288], F32)
    P0SEG_END = AR.off

    p0_pieces = []
    w1src = w1_d.rearrange("(c p) f -> p c f", p=128)
    w2src = w2_d.rearrange("(c p) f -> p c f", p=128)
    for rb in range(HFF // 128):
        for lo in (0, 1024):
            p0_pieces.append((w1src[:, rb, lo:lo + 1024], 1024,
                              w1q_v[:, rb, lo:lo + 1024], 0))
        for lo in (0, 1024):
            p0_pieces.append((w2src[:, rb, lo:lo + 1024], 1024,
                              w2q_v[:, rb, lo:lo + 1024], 1))
    w3src = w3_d.rearrange("(c p) f -> p c f", p=128)
    for rb in range(D // 128):
        for lo in range(0, HFF, 1024):
            hi = min(HFF, lo + 1024)
            p0_pieces.append((w3src[:, rb, lo:hi], hi - lo, w3q_v[:, rb, lo:hi], 2))
    NP0 = len(p0_pieces)
    p0_wcols = {}
    p0_abscol = []
    _wctr = {}
    _wbase = {0: 0, 1: 88, 2: 176}
    for idx, (_, n, _, wi) in enumerate(p0_pieces):
        p0_wcols.setdefault(wi, []).append(idx)
        c = _wbase[wi] + _wctr.get(wi, 0)
        _wctr[wi] = _wctr.get(wi, 0) + 1
        p0_abscol.append(c)
    p0_state = {"g": 0, "q": 0, "ghist": {}, "qhist": {}, "thr_done": 0,
                "wsrec": {}, "glv": {}, "qlv": {}}

    def p0_load_gamma(i):
        srcap, n, _, _ = p0_pieces[i]
        b = i % 4
        return dma("scalar", f"p0l{b}", p0_tile[b][:, 0:n], srcap,
                   waits=[("sc", p0_state["ghist"].get(i - 4, 0))])

    def p0_pump_gamma(k):
        for _ in range(k):
            i = p0_state["g"]
            if i >= NP0:
                return
            p0_state["g"] += 1
            if i == 0:
                for j in range(4):
                    p0_state["glv"][j] = p0_load_gamma(j)
            elif i + 3 < NP0:
                p0_state["glv"][i + 3] = p0_load_gamma(i + 3)
            srcap, n, _, wi = p0_pieces[i]
            b = i % 4
            ac = p0_abscol[i]
            sop(lambda h, t=p0_tile[b][:, 0:n], o=p0_abs[:, ac:ac + 1], nn=n:
                h.activation(trash[:, 0:nn], t, AF.Abs, accum_out=o),
                waits=[(f"p0l{b}", p0_state["glv"][i])])
            p0_state["ghist"][i] = smark()

    def p0_thr():
        assert p0_state["g"] == NP0, "gamma pump incomplete"
        tot = AR.take([128, 4], F32)
        th11 = AR.take([128, 2], F32)
        c_gam = p0_state["ghist"][NP0 - 1]
        _ranges = {0: (0, 88), 1: (88, 176), 2: (176, 272)}
        for wi in range(3):
            cols = p0_wcols[wi]
            nelem = float(sum(p0_pieces[c][1] for c in cols) * 128)
            r0, r1 = _ranges[wi]
            vop(lambda h, c0=r0, c1=r1, o=tot[:, 0:1]:
                h.reduce_sum(o, p0_abs[:, c0:c1], axis=AX.X),
                waits=[("sc", c_gam)], inc=("vec", 1))
            mm(PS[7][0:1, 0:1], ones_fcol[:], tot[:, 0:1], True, True,
               waits=[("vec", pr.c("vec"))] + psw([7]))
            gid = mm_end()
            vop(lambda h, n=nelem: h.tensor_scalar(th11[0:1, 0:1], PS[7][0:1, 0:1],
                                                   0.5 / n, 0.5e-8, ALU.mult, ALU.add),
                waits=[("mm", gid)], inc=("vec", 1))
            ps_free[7] = ("vec", pr.c("vec"))
            mm(PS[7][:, 0:1], ones_frow[:], th11[0:1, 0:1], True, True,
               waits=[("vec", pr.c("vec"))] + psw([7]))
            gid = mm_end()
            vop(lambda h, o=thr_sb[:, 2 * wi:2 * wi + 1]: h.tensor_copy(o, PS[7][:, 0:1]),
                waits=[("mm", gid)])
            vop(lambda h, o=thr_sb[:, 2 * wi + 1:2 * wi + 2]:
                h.tensor_scalar(o, PS[7][:, 0:1], -1.0, None, ALU.mult))
            cv = vmark()
            ps_free[7] = ("vec", cv)
        p0_state["thr_done"] = pr.c("vec")

    def p0_load_quant(i):
        srcap, n, _, _ = p0_pieces[i]
        b = i % 2
        return dma("gpsimd", f"p0q{b}", p0_tile[b][:, 0:n], srcap,
                   waits=[("vec", p0_state["qhist"].get(i - 2, 0))])

    def p0_pump_quant(k):
        if p0_state["thr_done"] == 0:
            return
        geb, leb = p0_tile[2], p0_tile[3]
        for _ in range(k):
            i = p0_state["q"]
            if i >= NP0:
                return
            p0_state["q"] += 1
            if i == 0:
                for j in range(2):
                    p0_state["qlv"][j] = p0_load_quant(j)
            elif i + 1 < NP0:
                p0_state["qlv"][i + 1] = p0_load_quant(i + 1)
            srcap, n, dstap, wi = p0_pieces[i]
            b = i % 2
            w1st = [(f"p0q{b}", p0_state["qlv"][i]), ("vec", p0_state["thr_done"])]
            if i == 0:
                w1st.append(("sc", p0_state["ghist"][NP0 - 1]))
            vop(lambda h, t=p0_tile[b][:, 0:n], o=geb[:, 0:n],
                th=thr_sb[:, 2 * wi:2 * wi + 1]:
                h.tensor_scalar(o, t, th, None, ALU.is_ge),
                waits=w1st)
            vop(lambda h, t=p0_tile[b][:, 0:n], o=leb[:, 0:n],
                th=thr_sb[:, 2 * wi + 1:2 * wi + 2]:
                h.tensor_scalar(o, t, th, -1.0, ALU.is_le, ALU.mult))
            vspace()
            vop(lambda h, a=geb[:, 0:n], c=leb[:, 0:n], o=p0_wq[b][:, 0:n]:
                h.tensor_tensor(o, a, c, ALU.add),
                waits=[(f"ws{b}", pr.c(f"ws{b}"))])
            p0_state["qhist"][i] = vmark()
            dma("gpsimd", f"ws{b}", dstap, p0_wq[b][:, 0:n],
                waits=[("vec", p0_state["qhist"][i])])
            p0_state["wsrec"][i] = (pr.c("ws0"), pr.c("ws1"))

    def p0_quant_upto(piece):
        while p0_state["q"] <= piece:
            assert p0_state["thr_done"] > 0, "quant needed before thr ready"
            p0_pump_quant(8)
        return [("ws0", p0_state["wsrec"][piece][0]),
                ("ws1", p0_state["wsrec"][piece][1])]

    def p1a():
        a0 = P0SEG_END
        AR.reset(a0)
        TCH = 4
        xt = [AR.take([128, TCH, D], F32) for _ in range(2)]
        xnt = [AR.take([128, TCH, D], BF16) for _ in range(2)]
        st = AR.take([128, 8], F32)
        v_hist = {}
        for ti in range(NCHUNK // TCH):
            b = ti % 2
            lv = dma("sync", f"x{b}", xt[b][:], x_v[:, ti * TCH:(ti + 1) * TCH, :],
                     waits=[("vec", v_hist.get(ti - 2, 0))])
            for j in range(TCH):
                xc = xt[b][:, j, :]
                vop(lambda h, i=xc: h.reduce_sum(st[:, 0:1], i, axis=AX.X),
                    waits=[(f"x{b}", lv)])
                sop(lambda h, i=xc: h.activation(trash[:, 0:D], i, AF.Square,
                                                 accum_out=st[:, 1:2]),
                    waits=[(f"x{b}", lv), ("vec", pr.c("vec") + 0)], inc=("sc", 1))
                c_sq = pr.c("sc")
                vspace()
                vop(lambda h: h.tensor_scalar(st[:, 2:3], st[:, 0:1], 1.0 / D, None, ALU.mult))
                vop(lambda h: h.tensor_scalar(st[:, 3:4], st[:, 1:2], 1.0 / D, EPS_LN,
                                              ALU.mult, ALU.add),
                    waits=[("sc", c_sq)])
                vop(lambda h: h.tensor_mul(st[:, 4:5], st[:, 2:3], st[:, 2:3]))
                vspace()
                vop(lambda h: h.tensor_sub(st[:, 5:6], st[:, 3:4], st[:, 4:5]),
                    inc=("vec", 1))
                sop(lambda h: h.activation(st[:, 6:7], st[:, 5:6], AF.Sqrt),
                    waits=[("vec", pr.c("vec"))], inc=("sc", 1))
                vop(lambda h: h.reciprocal(st[:, 7:8], st[:, 6:7]),
                    waits=[("sc", pr.c("sc"))])
                vspace()
                vop(lambda h, i=xc, o=xnt[b][:, j, :]:
                    h.tensor_scalar(o, i, st[:, 2:3], st[:, 7:8], ALU.subtract, ALU.mult),
                    waits=[(f"xs{b}", pr.c(f"xs{b}"))],
                    inc=("vec", 1))
            v_hist[ti] = pr.c("vec")
            dma("gpsimd", f"xs{b}", xn_v[:, ti * TCH:(ti + 1) * TCH, :], xnt[b][:],
                waits=[("vec", v_hist[ti])])
            p0_pump_gamma(3)
        AR.reset(a0)

    def p1b():
        a0 = P0SEG_END
        AR.reset(a0)
        wq_sb = AR.take([128, DC, NH * HD], BF16)
        wk_sb = AR.take([128, DC, NKV * HD], BF16)
        wv_sb = AR.take([128, DC, NKV * HD], BF16)
        xnT = [AR.take([128, DC, 256], BF16)]
        q_tm = AR.take([128, NH, HD], F32)
        k_tm = AR.take([128, NKV, HD], F32)
        q_out = [AR.take([128, NH, HD], BF16) for _ in range(2)]
        k_out = [AR.take([128, NKV, HD], BF16) for _ in range(2)]
        v_bf = [AR.take([128, NKV * HD], BF16) for _ in range(2)]
        ropeb = [AR.take([128, 2 * ROT], F32) for _ in range(2)]
        ropebf = AR.take([128, 2 * ROT], BF16)
        tmp3 = AR.take([128, NH, ROT], BF16)
        prodA = AR.take([128, NH, ROT], BF16)
        ssq = AR.take([128, NH], F32)
        inv = AR.take([128, NH], F32)

        dma("gpsimd", "wres", wq_sb[:], wq_d.rearrange("(c p) f -> p c f", p=128))
        dma("gpsimd", "wres", wk_sb[:], wk_d.rearrange("(c p) f -> p c f", p=128))
        wres_v = dma("gpsimd", "wres", wv_sb[:], wv_d.rearrange("(c p) f -> p c f", p=128))

        def rms_rope(src3, nhead, out3, bcscale, ropetile, first_waits):
            src_all = src3[:, 0:nhead, :]
            srcR = src3[:, 0:nhead, 0:ROT]
            srcE = src3[:, 0:nhead, 0:ROT:2]
            srcO = src3[:, 0:nhead, 1:ROT:2]
            srcHI = src3[:, 0:nhead, ROT:HD]
            inv_bc = inv[:, 0:nhead].rearrange("p (h o) -> p h o", o=1).to_broadcast([128, nhead, HD])
            sc_bc = bcscale[:, 0:HD].rearrange("p (o f) -> p o f", o=1).to_broadcast([128, nhead, HD])
            cos_bc = ropetile[:, 0:ROT].rearrange("p (o f) -> p o f", o=1).to_broadcast([128, nhead, ROT])
            sinb_bc = ropebf[:, ROT:2 * ROT].rearrange("p (o f) -> p o f", o=1).to_broadcast([128, nhead, ROT])
            tmpN = tmp3[:, 0:nhead, :]
            tmpE = tmp3[:, 0:nhead, 0:ROT:2]
            tmpO = tmp3[:, 0:nhead, 1:ROT:2]
            prodN = prodA[:, 0:nhead, :]
            outR = out3[:, 0:nhead, 0:ROT]
            outHI = out3[:, 0:nhead, ROT:HD]
            for hh in range(nhead):
                sop(lambda h, i=src3[:, hh, :], o=ssq[:, hh:hh + 1]:
                    h.activation(trash[:, 0:HD], i, AF.Square, accum_out=o),
                    waits=first_waits if hh == 0 else ())
            c1 = smark()
            vop(lambda h, n=nhead: h.tensor_scalar(inv[:, 0:n], ssq[:, 0:n],
                                                   1.0 / HD, EPS_RMS, ALU.mult, ALU.add),
                waits=[("sc", c1)], inc=("vec", 1))
            sop(lambda h, n=nhead: h.activation(inv[:, 0:n], inv[:, 0:n], AF.Sqrt),
                waits=[("vec", pr.c("vec"))], inc=("sc", 1))
            vop(lambda h, n=nhead: h.reciprocal(inv[:, 0:n], inv[:, 0:n]),
                waits=[("sc", pr.c("sc"))])
            vspace()
            vop(lambda h, a=src_all, b=inv_bc: h.tensor_mul(a, a, b))
            vspace()
            vop(lambda h, a=src_all, b=sc_bc: h.tensor_mul(a, a, b))
            vspace()
            vop(lambda h, o=tmpE, i=srcO: h.tensor_copy(o, i))
            vop(lambda h, o=tmpO, i=srcE: h.tensor_copy(o, i))
            vop(lambda h, o=prodN, i=srcR, c=cos_bc: h.tensor_mul(o, i, c))
            vspace()
            vop(lambda h, a=tmpN, c=sinb_bc: h.tensor_mul(a, a, c))
            vop(lambda h, o=outHI, i=srcHI: h.tensor_copy(o, i))
            vop(lambda h, o=outR, a=prodN, b=tmpN: h.tensor_add(o, a, b))
            return vmark()

        mm_hist = {}
        rope_hist = {}
        for st_i in range(NCHUNK // 2):
            sb = 0
            xv = 0
            for dc in range(DC):
                xv = dma("sync", f"xt{sb}",
                         xnT[sb][:, dc, :],
                         xn_s[st_i * 256:(st_i + 1) * 256, dc * 128:(dc + 1) * 128],
                         waits=[("mm", mm_hist.get(st_i - 1, 0))] if dc == 0 else (),
                         transpose=True)
            for tj in range(2):
                c = st_i * 2 + tj
                rb = c % 2
                pb = c % 2
                rv = dma("sync", f"rp{rb}", ropeb[rb][:], rope_d[c],
                         waits=[("vec", rope_hist.get(c - 2, 0))])
                vop(lambda h, i=ropeb[rb][:, ROT:2 * ROT], o=ropebf[:, ROT:2 * ROT]:
                    h.tensor_copy(o, i), waits=[(f"rp{rb}", rv)])
                qw = psw([0, 1, 2, 3]) + [(f"xt{sb}", xv), ("wres", wres_v)]
                for dc in range(DC):
                    lhs = xnT[sb][:, dc, tj * 128:(tj + 1) * 128]
                    for blk in range(4):
                        mm(PS[blk][:], lhs, wq_sb[:, dc, blk * 512:(blk + 1) * 512],
                           dc == 0, dc == DC - 1,
                           waits=qw if (dc == 0 and blk == 0) else ())
                g_q = mm_end()
                kw_ = psw([4, 5])
                for dc in range(DC):
                    lhs = xnT[sb][:, dc, tj * 128:(tj + 1) * 128]
                    for blk in range(2):
                        mm(PS[4 + blk][:], lhs, wk_sb[:, dc, blk * 512:(blk + 1) * 512],
                           dc == 0, dc == DC - 1,
                           waits=kw_ if (dc == 0 and blk == 0) else ())
                g_k = mm_end()
                vw_ = psw([6, 7])
                for dc in range(DC):
                    lhs = xnT[sb][:, dc, tj * 128:(tj + 1) * 128]
                    for blk in range(2):
                        mm(PS[6 + blk][:], lhs, wv_sb[:, dc, blk * 512:(blk + 1) * 512],
                           dc == 0, dc == DC - 1,
                           waits=vw_ if (dc == 0 and blk == 0) else ())
                g_v = mm_end()
                q2 = q_tm.rearrange("p a b -> p (a b)")
                k2 = k_tm.rearrange("p a b -> p (a b)")
                for blk in range(4):
                    vop(lambda h, o=q2[:, blk * 512:(blk + 1) * 512], i=PS[blk][:]:
                        h.tensor_copy(o, i),
                        waits=[("mm", g_q)] if blk == 0 else ())
                cq = vmark()
                for blk in range(4):
                    ps_free[blk] = ("vec", cq)
                for blk in range(2):
                    vop(lambda h, o=k2[:, blk * 512:(blk + 1) * 512], i=PS[4 + blk][:]:
                        h.tensor_copy(o, i),
                        waits=[("mm", g_k)] if blk == 0 else ())
                ck = vmark()
                for blk in range(4, 6):
                    ps_free[blk] = ("vec", ck)
                for blk in range(2):
                    vop(lambda h, o=v_bf[pb][:, blk * 512:(blk + 1) * 512], i=PS[6 + blk][:]:
                        h.tensor_copy(o, i),
                        waits=[("mm", g_v), (f"vs{pb}", pr.c(f"vs{pb}"))] if blk == 0 else ())
                cv0 = vmark()
                for blk in range(6, 8):
                    ps_free[blk] = ("vec", cv0)
                dma("gpsimd", f"vs{pb}", v_v[:, c, :], v_bf[pb][:],
                    waits=[("vec", cv0)])
                pr.emit("vector", lambda h: h.nop(),
                        waits=[(f"qs{pb}", pr.c(f"qs{pb}")), (f"ks{pb}", pr.c(f"ks{pb}")),
                               (f"rp{rb}", rv)])
                rms_rope(q_tm, NH, q_out[pb], qs_bc, ropeb[rb], [("vec", cq)])
                cv = rms_rope(k_tm, NKV, k_out[pb], ks_bc, ropeb[rb], [("vec", pr.c("vec"))])
                rope_hist[c] = cv
                dma("gpsimd", f"qs{pb}", q_v[:, c, :],
                    q_out[pb].rearrange("p a b -> p (a b)"),
                    waits=[("vec", cv)])
                dma("gpsimd", f"ks{pb}", k_v[:, c, :],
                    k_out[pb].rearrange("p a b -> p (a b)"),
                    waits=[("vec", cv)])
                p0_pump_gamma(4)
            mm_hist[st_i] = pr.c("mm")

        AR.reset(a0)

    def p2():
        a0 = P0SEG_END
        AR.reset(a0)
        GC2 = 4
        NG2 = NCHUNK // GC2
        GQ2 = GC2 * 128
        wo_sb = AR.take([128, DC, D], BF16)
        kar = [AR.take([128, 2, 1536], BF16) for _ in range(2)]
        var_ = [AR.take([128, 12, HD], BF16) for _ in range(2)]
        qar = [AR.take([128, 2, GQ2], BF16) for _ in range(2)]
        PT = AR.take([128, 2, GQ2], BF16)
        tanh_sb = AR.take([128, GQ2], F32)
        linv = AR.take([128, GQ2], F32)
        rbc = AR.take([128, GQ2], F32)
        attn_sb = AR.take([128, DC, GQ2], BF16)
        s_st = [AR.take([128, D], F32) for _ in range(2)]

        wres_v = dma("gpsimd", "wres", wo_sb[:], wo_d.rearrange("(c p) f -> p c f", p=128))
        k_hist = {}
        q_hist = {}

        for g in range(NG2):
            c0 = g * GC2
            kbase = max(0, c0 - 8)
            nkc = c0 + 4 - kbase
            kcs = list(range(kbase, c0 + 4))
            for kvh in range(NKV):
                kb = kvh % 2
                kwaits = [("mm", k_hist.get(kb, 0))]
                kv_ = 0
                for hdc in range(2):
                    kv_ = dma("sync", f"ka{kb}", kar[kb][:, hdc, 0:nkc * 128],
                              k_s[kbase * 128:(c0 + 4) * 128,
                                  kvh * HD + hdc * 128:kvh * HD + (hdc + 1) * 128],
                              waits=kwaits if hdc == 0 else (), transpose=True)
                vv_ = dma("sync", f"va{kb}", var_[kb][:, 0:nkc, :],
                          v_v[:, kbase:c0 + 4, kvh * HD:(kvh + 1) * HD])
                for qh in range(2):
                    head = kvh * 2 + qh
                    qb = head % 2
                    qv_ = 0
                    for hdc in range(2):
                        qv_ = dma("sync", f"qa{qb}", qar[qb][:, hdc, :],
                                  q_s[c0 * 128:(c0 + GC2) * 128,
                                      head * HD + hdc * 128:head * HD + (hdc + 1) * 128],
                                  waits=[("mm", q_hist.get(qb, 0))] if hdc == 0 else (),
                                  transpose=True)
                    atb = [3, 4] if qh == 0 else [5, 6]
                    l_first = True
                    at_first = [True, True]
                    last_kc = kcs[-1]
                    pvmm = {}
                    c_exp = {}
                    c_mask = {}
                    tanh_c = {}

                    def emit_s(kc):
                        qlo = max(kc, c0)
                        qhi = min(kc + 8, c0 + GC2 - 1)
                        ncols = (qhi - qlo + 1) * 128
                        qoff = (qlo - c0) * 128
                        kcol = (kc - kbase) * 128
                        par = kc % 2
                        sw = psw([par]) + [(f"ka{kb}", kv_), (f"qa{qb}", qv_)]
                        for hdc in range(2):
                            mm(PS[par][:, 0:ncols],
                               kar[kb][:, hdc, kcol:kcol + 128],
                               qar[qb][:, hdc, qoff:qoff + ncols],
                               hdc == 0, hdc == 1,
                               waits=sw if hdc == 0 else ())
                        g_s = pr.mark_last("tensor", "mm")
                        sop(lambda h, o=tanh_sb[:, 0:ncols], i=PS[par][:, 0:ncols]:
                            h.activation(o, i, AF.Tanh, scale=1.0 / CAP),
                            waits=[("mm", g_s)])
                        tanh_c[kc] = smark()
                        ps_free[par] = ("sc", tanh_c[kc])
                        sop(lambda h, o=PT[:, par, 0:ncols], i=tanh_sb[:, 0:ncols]:
                            h.activation(o, i, AF.Exp, scale=CAP),
                            waits=[("mm", pvmm.get(kc - 2, 0))])
                        c_exp[kc] = smark()
                        nm = 0
                        if kc >= c0:
                            vop(lambda h, o=PT[:, par, 0:128]:
                                h.tensor_mul(o, o, mask_sb[:, 0:128]),
                                waits=[("sc", c_exp[kc])])
                            nm += 1
                        if kc + 8 <= c0 + GC2 - 1:
                            off = (kc + 8 - qlo) * 128
                            vop(lambda h, o=PT[:, par, off:off + 128]:
                                h.tensor_mul(o, o, mask_sb[:, 128:256]),
                                waits=[("sc", c_exp[kc])] if nm == 0 else ())
                            nm += 1
                        if nm > 0:
                            c_mask[kc] = (("vec", pr.mark_last("vector", "vec")),)
                        else:
                            c_mask[kc] = (("sc", c_exp[kc]),)

                    def emit_pv(kc):
                        nonlocal l_first
                        qlo = max(kc, c0)
                        qhi = min(kc + 8, c0 + GC2 - 1)
                        ncols = (qhi - qlo + 1) * 128
                        qoff = (qlo - c0) * 128
                        par = kc % 2
                        lw = list(c_mask[kc])
                        w_ = lw + (psw([2]) if l_first else [])
                        mm(PS[2][0:1, qoff:qoff + ncols],
                           ones_bcol[:], PT[:, par, 0:ncols],
                           l_first, kc == last_kc, waits=w_)
                        l_first = False
                        for hdc in range(2):
                            bi = atb[hdc]
                            w_ = lw + [(f"va{kb}", vv_)]
                            if at_first[hdc]:
                                w_ = w_ + psw([bi])
                            mm(PS[bi][:, qoff:qoff + ncols],
                               var_[kb][:, kc - kbase, hdc * 128:(hdc + 1) * 128],
                               PT[:, par, 0:ncols],
                               at_first[hdc], kc == last_kc, waits=w_)
                            at_first[hdc] = False
                        pvmm[kc] = pr.mark_last("tensor", "mm")

                    for idx, kc in enumerate(kcs):
                        emit_s(kc)
                        if idx > 0:
                            emit_pv(kcs[idx - 1])
                    emit_pv(last_kc)
                    g_all = pvmm[last_kc]
                    vop(lambda h: h.reciprocal(linv[0:1, 0:GQ2], PS[2][0:1, 0:GQ2]),
                        waits=[("mm", g_all)])
                    cv = vmark()
                    ps_free[2] = ("vec", cv)
                    mm(PS[7][:, 0:GQ2], ones_frow[:], linv[0:1, 0:GQ2], True, True,
                       waits=[("vec", cv)] + psw([7]))
                    g_bc = mm_end()
                    vop(lambda h: h.tensor_copy(rbc[:, 0:GQ2], PS[7][:, 0:GQ2]),
                        waits=[("mm", g_bc)])
                    cv = vmark()
                    ps_free[7] = ("vec", cv)
                    vspace()
                    for hdc in range(2):
                        bi = atb[hdc]
                        vop(lambda h, o=attn_sb[:, head * 2 + hdc, :],
                            i=PS[bi][:, 0:GQ2], r=rbc[:, 0:GQ2]:
                            h.tensor_mul(o, i, r))
                    cv = vmark()
                    for bi in atb:
                        ps_free[bi] = ("vec", cv)
                    q_hist[qb] = g_all
                    p0_pump_gamma(4)
                k_hist[kb] = pr.c("mm")
            c_attn = pr.c("vec")
            for qt in range(GC2):
                c = c0 + qt
                par = qt % 2
                banks = [4 * par + i for i in range(4)]
                w0 = psw(banks) + [("vec", c_attn), ("wres", wres_v)]
                for hdc in range(DC):
                    lhs = attn_sb[:, hdc, qt * 128:(qt + 1) * 128]
                    for dcb in range(4):
                        mm(PS[banks[dcb]][:], lhs, wo_sb[:, hdc, dcb * 512:(dcb + 1) * 512],
                           hdc == 0, hdc == DC - 1,
                           waits=w0 if (hdc == 0 and dcb == 0) else ())
                g_wo = mm_end()
                for dcb in range(4):
                    vop(lambda h, o=s_st[par][:, dcb * 512:(dcb + 1) * 512], i=PS[banks[dcb]][:]:
                        h.tensor_copy(o, i),
                        waits=[("mm", g_wo), (f"ss{par}", pr.c(f"ss{par}"))] if dcb == 0 else ())
                cv = vmark()
                for b in banks:
                    ps_free[b] = ("vec", cv)
                dma("gpsimd", f"ss{par}", s_v[:, c, :], s_st[par][:],
                    waits=[("vec", cv)])
        while p0_state["g"] < NP0:
            p0_pump_gamma(4)
        p0_thr()
        AR.reset(a0)

    def p3():
        a0 = P0SEG_END
        AR.reset(a0)
        xnf = AR.take([128, DC, 2048], BF16)
        w1T = [AR.take([128, DC, 256], BF16) for _ in range(2)]
        w2T = [AR.take([128, DC, 256], BF16) for _ in range(2)]
        sig = AR.take([128, D], F32)
        hidst = [AR.take([128, D], BF16) for _ in range(2)]

        NS = HFF // 256
        mm_hist = {}
        c_tt_prev = 0
        it = 0
        for half in range(2):
            hb = half * 2048
            xv = 0
            for dc in range(DC):
                xv = dma("sync", "xnf", xnf[:, dc, :],
                         xn_s[hb * 1:hb + 2048, dc * 128:(dc + 1) * 128],
                         waits=[("mm", pr.c("mm"))] if dc == 0 else (),
                         transpose=True)
            xnf_done = xv
            for hs in range(NS):
                sb = hs % 2
                wv1 = wv2 = 0
                wqdep = p0_quant_upto(8 * hs + 7)
                wqdep2 = wqdep
                for dc in range(DC):
                    wv1 = dma("sync", f"w1t{sb}", w1T[sb][:, dc, :],
                              w1q_s[hs * 256:(hs + 1) * 256, dc * 128:(dc + 1) * 128],
                              waits=(wqdep + [("mm", mm_hist.get((half, hs - 2), 0))])
                              if dc == 0 else (),
                              transpose=True)
                    wv2 = dma("sync", f"w2t{sb}", w2T[sb][:, dc, :],
                              w2q_s[hs * 256:(hs + 1) * 256, dc * 128:(dc + 1) * 128],
                              waits=wqdep2 if dc == 0 else (),
                              transpose=True)
                for hcl in range(2):
                    hc = hs * 2 + hcl
                    w_ = psw([0, 1, 2, 3]) + [("xnf", xnf_done), (f"w1t{sb}", wv1)]
                    for dc in range(DC):
                        lhs = w1T[sb][:, dc, hcl * 128:(hcl + 1) * 128]
                        for tb in range(4):
                            mm(PS[tb][:], lhs, xnf[:, dc, tb * 512:(tb + 1) * 512],
                               dc == 0, dc == DC - 1,
                               waits=w_ if (dc == 0 and tb == 0) else ())
                    g_1 = mm_end()
                    w_ = psw([4, 5, 6, 7]) + [(f"w2t{sb}", wv2)]
                    for dc in range(DC):
                        lhs = w2T[sb][:, dc, hcl * 128:(hcl + 1) * 128]
                        for tb in range(4):
                            mm(PS[4 + tb][:], lhs, xnf[:, dc, tb * 512:(tb + 1) * 512],
                               dc == 0, dc == DC - 1,
                               waits=w_ if (dc == 0 and tb == 0) else ())
                    g_2 = mm_end()
                    for tb in range(4):
                        sop(lambda h, o=sig[:, tb * 512:(tb + 1) * 512], i=PS[tb][:]:
                            h.activation(o, i, AF.Sigmoid),
                            waits=[("mm", g_1), ("vec", c_tt_prev)] if tb == 0 else ())
                    c_sg = smark()
                    for tb in range(4):
                        ps_free[tb] = ("sc", c_sg)
                    hp = it % 2
                    it += 1
                    for tb in range(4):
                        vop(lambda h, o=hidst[hp][:, tb * 512:(tb + 1) * 512],
                            i=PS[4 + tb][:], s_=sig[:, tb * 512:(tb + 1) * 512]:
                            h.tensor_mul(o, i, s_),
                            waits=[("mm", g_2), ("sc", c_sg),
                                   (f"hs{hp}", pr.c(f"hs{hp}"))] if tb == 0 else ())
                    c_tt_prev = vmark()
                    for tb in range(4, 8):
                        ps_free[tb] = ("vec", c_tt_prev)
                    dma("gpsimd", f"hs{hp}", hid_v[:, hc, hb:hb + 2048], hidst[hp][:],
                        waits=[("vec", c_tt_prev)])
                    p0_pump_quant(4)
                mm_hist[(half, hs)] = pr.c("mm")
        while p0_state["q"] < NP0:
            p0_pump_quant(8)
        AR.reset(a0)

    def p4():
        a0 = P0SEG_END
        AR.reset(a0)
        w3T = AR.take([128, HC, 1024], BF16)
        hid = [AR.take([128, HC, 256], BF16) for _ in range(2)]
        g_sb = AR.take([128, 1024], F32)
        s_t = [AR.take([128, 1024], F32) for _ in range(2)]
        x_t = [AR.take([128, 1024], F32) for _ in range(2)]
        o_t = [AR.take([128, 1024], F32) for _ in range(2)]

        c_tt_prev = 0
        for dh in range(2):
            w3_done = 0
            wqdep = p0_quant_upto(4 * HC + (8 * dh + 8) * 6 - 1)
            for hc in range(HC):
                w3_done = dma("sync", "w3t", w3T[:, hc, :],
                              w3q_s[dh * 1024:(dh + 1) * 1024, hc * 128:(hc + 1) * 128],
                              waits=(wqdep + [("mm", pr.c("mm"))]) if hc == 0 else (),
                              transpose=True)
            mm_hist = {}
            v_hist = {}
            for tb in range(16):
                sb = tb % 2
                hv = dma("sync", f"hl{sb}", hid[sb][:],
                         hid_v[:, :, tb * 256:(tb + 1) * 256],
                         waits=[("mm", mm_hist.get(tb - 2, 0))])
                for tcl in range(2):
                    c = tb * 2 + tcl
                    pb = c % 2
                    dma("sync", f"sx{pb}", s_t[pb][:],
                        s_v[:, c, dh * 1024:(dh + 1) * 1024],
                        waits=[("vec", v_hist.get(c - 2, 0))])
                    sxv = dma("sync", f"sx{pb}", x_t[pb][:],
                              x_v[:, c, dh * 1024:(dh + 1) * 1024])
                    banks = [2 * pb, 2 * pb + 1]
                    w_ = psw(banks) + [(f"hl{sb}", hv), ("w3t", w3_done)]
                    for hc in range(HC):
                        lhs = hid[sb][:, hc, tcl * 128:(tcl + 1) * 128]
                        for dcb in range(2):
                            mm(PS[banks[dcb]][:], lhs, w3T[:, hc, dcb * 512:(dcb + 1) * 512],
                               hc == 0, hc == HC - 1,
                               waits=w_ if (hc == 0 and dcb == 0) else ())
                    g_f = mm_end()
                    for dcb in range(2):
                        sop(lambda h, o=g_sb[:, dcb * 512:(dcb + 1) * 512], i=PS[banks[dcb]][:]:
                            h.activation(o, i, AF.Silu),
                            waits=[("mm", g_f), ("vec", c_tt_prev)] if dcb == 0 else ())
                    c_sl = smark()
                    for b in banks:
                        ps_free[b] = ("sc", c_sl)
                    vop(lambda h, o=o_t[pb], a=g_sb, s_=s_t[pb]:
                        h.tensor_mul(o, a, s_),
                        waits=[("sc", c_sl), (f"sx{pb}", sxv),
                               (f"os{pb}", pr.c(f"os{pb}"))])
                    vspace()
                    vop(lambda h, o=o_t[pb], b_=x_t[pb]: h.tensor_add(o, o, b_))
                    c_tt_prev = vmark()
                    v_hist[c] = c_tt_prev
                    dma("gpsimd", f"os{pb}", out_v[:, c, dh * 1024:(dh + 1) * 1024],
                        o_t[pb], waits=[("vec", c_tt_prev)])
                mm_hist[tb] = pr.c("mm")
        AR.reset(a0)

    if "setup" in phases or not phases:
        pass
    if "nosetup" not in phases:
        lvl = 4
        for p in phases:
            if p.startswith("lvl"):
                lvl = int(p[3:])
        setup(lvl)
    pr.barrier()
    if "p1a" in phases:
        p1a()
        pr.barrier()
    if "p1b" in phases:
        p1b()
        pr.barrier()
    if "p2" in phases:
        p2()
        pr.barrier()
    if "p3" in phases:
        p3()
        pr.barrier()
    if "p4" in phases:
        p4()
    pr.barrier()
    for stn in ("os0", "os1"):
        if pr.c(stn):
            pr.emit("gpsimd", lambda h: h.nop(), waits=[(stn, pr.c(stn))])

    with nc.Block() as block:
        @block.sync
        def _(e):
            pr.replay("sync", e)

        @block.gpsimd
        def _(e):
            pr.replay("gpsimd", e)

        @block.tensor
        def _(e):
            pr.replay("tensor", e)

        @block.vector
        def _(e):
            pr.replay("vector", e)

        @block.scalar
        def _(e):
            pr.replay("scalar", e)

    return nc


def host_consts():
    theta = (1.0 / 10000.0 ** (np.arange(0, ROT, 2, dtype=np.float32) / ROT)).astype(np.float32)
    pos = np.arange(S, dtype=np.float32)
    ang = pos[:, None] * theta[None, :]
    cos2 = np.repeat(np.cos(ang).astype(np.float32), 2, axis=1)
    sin2 = np.repeat(np.sin(ang).astype(np.float32), 2, axis=1)
    sin2[:, 0::2] *= -1.0
    tab = np.concatenate([cos2, sin2], axis=1).reshape(NCHUNK, 128, 2 * ROT)
    k = np.arange(128)[:, None]
    q = np.arange(128)[None, :]
    diag = (q >= k).astype(ml_dtypes.bfloat16)
    far = (q < k).astype(ml_dtypes.bfloat16)
    masks = np.concatenate([diag, far], axis=1)
    return np.ascontiguousarray(tab), np.ascontiguousarray(masks)


_NC_CACHE = {}


def _get_nc(dump=False, phases=None):
    key = (dump, phases)
    if key not in _NC_CACHE:
        kw = {} if phases is None else {"phases": phases}
        _NC_CACHE[key] = build_nc(dump=dump, **kw)
    return _NC_CACHE[key]


def kernel(L0_attn_x, L0_attn_Wq, L0_attn_Wk, L0_attn_Wv, L0_attn_Wo,
           L0_attn_q_scale, L0_attn_k_scale, L0_ffn_w1, L0_ffn_w2, L0_ffn_w3,
           _dump=False, _phases=None):
    tab, masks = host_consts()
    x = np.ascontiguousarray(np.asarray(L0_attn_x, dtype=np.float32))
    base = {
        "wq": np.ascontiguousarray(np.asarray(L0_attn_Wq, np.float32)),
        "wk": np.ascontiguousarray(np.asarray(L0_attn_Wk, np.float32)),
        "wv": np.ascontiguousarray(np.asarray(L0_attn_Wv, np.float32)),
        "wo": np.ascontiguousarray(np.asarray(L0_attn_Wo, np.float32)),
        "qsc": np.ascontiguousarray(np.asarray(L0_attn_q_scale, np.float32).reshape(1, HD)),
        "ksc": np.ascontiguousarray(np.asarray(L0_attn_k_scale, np.float32).reshape(1, HD)),
        "w1": np.ascontiguousarray(np.asarray(L0_ffn_w1, np.float32)),
        "w2": np.ascontiguousarray(np.asarray(L0_ffn_w2, np.float32)),
        "w3": np.ascontiguousarray(np.asarray(L0_ffn_w3, np.float32)),
        "ropetab": tab,
        "maskbf": masks,
    }
    in_maps = [dict(base, x=np.ascontiguousarray(x[c])) for c in range(8)]
    nc = _get_nc(dump=_dump, phases=None if _phases is None else tuple(_phases))
    res = run_bass_kernel_spmd(nc, in_maps, core_ids=list(range(8)))
    if _dump:
        return res
    return np.stack([r["out"] for r in res.results], axis=0)


# version 11
# speedup vs baseline: 1.1710x; 1.1710x over previous
import numpy as np
import ml_dtypes
import concourse.bass as bass
import concourse.mybir as mybir
from concourse.bass_utils import run_bass_kernel_spmd

F32 = mybir.dt.float32
BF16 = mybir.dt.bfloat16
AF = mybir.ActivationFunctionType
ALU = mybir.AluOpType
AX = mybir.AxisListType

S = 4096
D = 2048
NH = 8
NKV = 4
HD = 256
ROT = 128
WIN = 1024
HFF = 5632
CAP = 50.0
SCALE = HD ** (-0.5)
EPS_LN = 1e-5
EPS_RMS = 1e-6

NCHUNK = S // 128
NG = 4
GC = 8
GQ = GC * 128
DC = D // 128
HC = HFF // 128

ENGINES = ("sync", "gpsimd", "tensor", "vector", "scalar")


class Prog:

    def __init__(self, nc):
        self.nc = nc
        self.q = {e: [] for e in ENGINES}
        self.sems = {}
        self.cnt = {}
        self.last_wait = {}
        self.bar_round = 0
        self.dma_engs = {}

    def sem(self, name):
        if name not in self.sems:
            self.sems[name] = self.nc.alloc_semaphore(name)
            self.cnt[name] = 0
        return name

    def emit(self, eng, builder, waits=(), inc=None):
        ws = []
        for s, v in waits:
            if v <= 0:
                continue
            self.sem(s)
            key = (eng, s)
            if v > self.last_wait.get(key, 0):
                self.last_wait[key] = v
                ws.append((s, v))
        if inc is not None:
            self.sem(inc[0])
            self.cnt[inc[0]] += inc[1]
        self.q[eng].append([builder, ws, inc])
        return self.cnt[inc[0]] if inc is not None else None

    def mark_last(self, eng, sem, n=1):
        entry = self.q[eng][-1]
        assert entry[2] is None, f"last op on {eng} already carries an inc"
        self.sem(sem)
        self.cnt[sem] += n
        entry[2] = (sem, n)
        return self.cnt[sem]

    def c(self, s):
        return self.cnt.get(s, 0)

    def barrier(self):
        self.bar_round += 1
        tgt = 5 * self.bar_round
        for stream, e in self.dma_engs.items():
            self.emit(e, lambda h: h.nop(), waits=[(stream, self.c(stream))])
        for e in ENGINES:
            self.emit(e, lambda h: h.drain(), inc=("bar", 1))
        for e in ENGINES:
            self.emit(e, lambda h: h.nop(), waits=[("bar", tgt)])

    def replay(self, eng, handle):
        for builder, ws, inc in self.q[eng]:
            for s, v in ws:
                handle.wait_ge(self.sems[s], v)
            ins = builder(handle)
            if inc is not None:
                ins.then_inc(self.sems[inc[0]], inc[1])


class Arena:
    def __init__(self, nc, name, nbytes):
        self.t = nc.alloc_sbuf_tensor(name, [128, nbytes], mybir.dt.uint8)
        self.nbytes = nbytes
        self.off = 0
        self.hwm = 0

    def take(self, shape, dt):
        assert shape[0] == 128
        esz = mybir.dt.size(dt)
        n = int(np.prod(shape[1:])) * esz
        off = (self.off + 63) & ~63
        assert off + n <= self.nbytes, f"arena overflow: {off}+{n} > {self.nbytes}"
        self.off = off + n
        self.hwm = max(self.hwm, self.off)
        v = self.t[:, off:off + n].bitcast(dt)
        if len(shape) == 2:
            return v
        names = " ".join(f"a{i}" for i in range(1, len(shape)))
        kw = {f"a{i}": shape[i] for i in range(1, len(shape))}
        return v.rearrange(f"p ({names}) -> p {names}", **kw)

    def reset(self, off=0):
        self.off = off


def ceil_div(a, b):
    return (a + b - 1) // b


def build_nc(dump=False, phases=("p1a", "p1b", "p2", "p3", "p4")):
    nc = bass.Bass("TRN2", target_bir_lowering=False)

    x_d = nc.dram_tensor("x", [S, D], F32, kind="ExternalInput").ap()
    wq_d = nc.dram_tensor("wq", [D, NH * HD], F32, kind="ExternalInput").ap()
    wk_d = nc.dram_tensor("wk", [D, NKV * HD], F32, kind="ExternalInput").ap()
    wv_d = nc.dram_tensor("wv", [D, NKV * HD], F32, kind="ExternalInput").ap()
    wo_d = nc.dram_tensor("wo", [NH * HD, D], F32, kind="ExternalInput").ap()
    qs_d = nc.dram_tensor("qsc", [1, HD], F32, kind="ExternalInput").ap()
    ks_d = nc.dram_tensor("ksc", [1, HD], F32, kind="ExternalInput").ap()
    w1_d = nc.dram_tensor("w1", [HFF, D], F32, kind="ExternalInput").ap()
    w2_d = nc.dram_tensor("w2", [HFF, D], F32, kind="ExternalInput").ap()
    w3_d = nc.dram_tensor("w3", [D, HFF], F32, kind="ExternalInput").ap()
    rope_d = nc.dram_tensor("ropetab", [NCHUNK, 128, 2 * ROT], F32, kind="ExternalInput").ap()
    mask_d = nc.dram_tensor("maskbf", [128, 256], BF16, kind="ExternalInput").ap()
    out_d = nc.dram_tensor("out", [S, D], F32, kind="ExternalOutput").ap()

    skind = "ExternalOutput" if dump else "Internal"
    xn_s = nc.dram_tensor("xn_s", [S, D], BF16, kind=skind).ap()
    q_s = nc.dram_tensor("q_s", [S, NH * HD], BF16, kind=skind).ap()
    k_s = nc.dram_tensor("k_s", [S, NKV * HD], BF16, kind=skind).ap()
    v_s = nc.dram_tensor("v_s", [S, NKV * HD], BF16, kind=skind).ap()
    s_s = nc.dram_tensor("s_s", [S, D], F32, kind=skind).ap()
    hid_s = nc.dram_tensor("hid_s", [HFF, S], BF16, kind=skind).ap()
    w1q_s = nc.dram_tensor("w1q_s", [HFF, D], BF16, kind=skind).ap()
    w2q_s = nc.dram_tensor("w2q_s", [HFF, D], BF16, kind=skind).ap()
    w3q_s = nc.dram_tensor("w3q_s", [D, HFF], BF16, kind=skind).ap()
    dbg_d = nc.dram_tensor("dbg", [128, 6144], F32, kind="ExternalOutput" if dump else "Internal").ap()

    x_v = x_d.rearrange("(c p) d -> p c d", p=128)
    out_v = out_d.rearrange("(c p) d -> p c d", p=128)
    xn_v = xn_s.rearrange("(c p) d -> p c d", p=128)
    q_v = q_s.rearrange("(c p) d -> p c d", p=128)
    k_v = k_s.rearrange("(c p) d -> p c d", p=128)
    v_v = v_s.rearrange("(c p) d -> p c d", p=128)
    s_v = s_s.rearrange("(c p) d -> p c d", p=128)
    hid_v = hid_s.rearrange("(c p) t -> p c t", p=128)
    w1q_v = w1q_s.rearrange("(c p) d -> p c d", p=128)
    w2q_v = w2q_s.rearrange("(c p) d -> p c d", p=128)
    w3q_v = w3q_s.rearrange("(c p) h -> p c h", p=128)

    pr = Prog(nc)

    PS = [nc.alloc_psum_tensor(f"ps{i}", [128, 512], F32) for i in range(8)]
    ps_free = [("mm", 0)] * 8

    mask_sb = nc.alloc_sbuf_tensor("mask_sb", [128, 256], BF16)
    qs_bc = nc.alloc_sbuf_tensor("qs_bc", [128, HD], F32)
    ks_bc = nc.alloc_sbuf_tensor("ks_bc", [128, HD], F32)
    qs_row = nc.alloc_sbuf_tensor("qs_row", [1, HD], F32)
    ks_row = nc.alloc_sbuf_tensor("ks_row", [1, HD], F32)
    ones_fcol = nc.alloc_sbuf_tensor("ones_fcol", [128, 1], F32)
    ones_frow = nc.alloc_sbuf_tensor("ones_frow", [1, 128], F32)
    ones_bcol = nc.alloc_sbuf_tensor("ones_bcol", [128, 1], BF16)
    thr_sb = nc.alloc_sbuf_tensor("thr_sb", [128, 6], F32)
    trash = nc.alloc_sbuf_tensor("trash_sb", [128, D], F32)
    spacer_sb = nc.alloc_sbuf_tensor("spacer_sb", [128, 1], F32)

    AR = Arena(nc, "arena", 194 * 1024)

    def dma(eng, stream, out, in_, waits=(), transpose=False):
        s = pr.sem(stream)
        pr.dma_engs[stream] = eng
        v = pr.emit(eng,
                    lambda h, o=out, i=in_, t=transpose: h.dma_start(out=o, in_=i, transpose=t),
                    waits=waits, inc=(s, 16))
        return v

    def mm(out, lhsT, rhs, start, stop, waits=()):
        pr.emit("tensor",
                lambda h, o=out, l=lhsT, r=rhs, st=start, sp=stop:
                h.matmul(o, l, r, start=st, stop=sp),
                waits=waits)

    def mm_end():
        return pr.mark_last("tensor", "mm")

    def vop(fn, waits=(), inc=None):
        return pr.emit("vector", fn, waits=waits, inc=inc)

    def sop(fn, waits=(), inc=None):
        return pr.emit("scalar", fn, waits=waits, inc=inc)

    def vmark():
        return pr.mark_last("vector", "vec")

    def vspace():
        pr.emit("vector", lambda h: h.tensor_copy(spacer_sb[:], ones_fcol[:]))

    def smark():
        return pr.mark_last("scalar", "sc")

    def psw(banks):
        return [ps_free[b] for b in banks]

    def setup(level=4):
        for t, val in ((ones_fcol, 1.0), (ones_frow, 1.0), (ones_bcol, 1.0)):
            pr.emit("vector", lambda h, a=t, v=val: h.memset(a[:], v))
        c_ones = vmark()
        if level < 1:
            return
        dma("sync", "c0", mask_sb[:], mask_d)
        vq = dma("sync", "c0", qs_row[:], qs_d)
        vk = dma("sync", "c0", ks_row[:], ks_d)
        if level < 2:
            return
        mm(PS[0][:, 0:HD], ones_frow[:], qs_row[:], True, True,
           waits=[("c0", vq), ("vec", c_ones)])
        mm(PS[1][:, 0:HD], ones_frow[:], ks_row[:], True, True,
           waits=[("c0", vk)])
        gid = mm_end()
        if level < 3:
            return
        if level == 9:
            vop(lambda h: h.nop(), waits=[("mm", gid)])
            return
        vop(lambda h: h.tensor_scalar(qs_bc[:], PS[0][:, 0:HD], float(SCALE), None, ALU.mult),
            waits=[("mm", gid)])
        if level < 4:
            cv = vmark()
            ps_free[0] = ("vec", cv)
            return
        vop(lambda h: h.tensor_copy(ks_bc[:], PS[1][:, 0:HD]))
        cv = vmark()
        ps_free[0] = ("vec", cv)
        ps_free[1] = ("vec", cv)

    p0_tile = [AR.take([128, 1024], F32) for _ in range(4)]
    p0_wq = [AR.take([128, 1024], BF16) for _ in range(2)]
    p0_abs = AR.take([128, 288], F32)
    P0SEG_END = AR.off

    p0_pieces = []
    w1src = w1_d.rearrange("(c p) f -> p c f", p=128)
    w2src = w2_d.rearrange("(c p) f -> p c f", p=128)
    for rb in range(HFF // 128):
        for lo in (0, 1024):
            p0_pieces.append((w1src[:, rb, lo:lo + 1024], 1024,
                              w1q_v[:, rb, lo:lo + 1024], 0))
        for lo in (0, 1024):
            p0_pieces.append((w2src[:, rb, lo:lo + 1024], 1024,
                              w2q_v[:, rb, lo:lo + 1024], 1))
    w3src = w3_d.rearrange("(c p) f -> p c f", p=128)
    for rb in range(D // 128):
        for lo in range(0, HFF, 1024):
            hi = min(HFF, lo + 1024)
            p0_pieces.append((w3src[:, rb, lo:hi], hi - lo, w3q_v[:, rb, lo:hi], 2))
    NP0 = len(p0_pieces)
    p0_wcols = {}
    p0_abscol = []
    _wctr = {}
    _wbase = {0: 0, 1: 88, 2: 176}
    for idx, (_, n, _, wi) in enumerate(p0_pieces):
        p0_wcols.setdefault(wi, []).append(idx)
        c = _wbase[wi] + _wctr.get(wi, 0)
        _wctr[wi] = _wctr.get(wi, 0) + 1
        p0_abscol.append(c)
    p0_state = {"g": 0, "q": 0, "ghist": {}, "qhist": {}, "thr_done": 0,
                "wsrec": {}, "glv": {}, "qlv": {}}

    def p0_load_gamma(i):
        srcap, n, _, _ = p0_pieces[i]
        b = i % 4
        return dma("scalar", f"p0l{b}", p0_tile[b][:, 0:n], srcap,
                   waits=[("sc", p0_state["ghist"].get(i - 4, 0))])

    def p0_pump_gamma(k):
        for _ in range(k):
            i = p0_state["g"]
            if i >= NP0:
                return
            p0_state["g"] += 1
            if i == 0:
                for j in range(4):
                    p0_state["glv"][j] = p0_load_gamma(j)
            elif i + 3 < NP0:
                p0_state["glv"][i + 3] = p0_load_gamma(i + 3)
            srcap, n, _, wi = p0_pieces[i]
            b = i % 4
            ac = p0_abscol[i]
            sop(lambda h, t=p0_tile[b][:, 0:n], o=p0_abs[:, ac:ac + 1], nn=n:
                h.activation(trash[:, 0:nn], t, AF.Abs, accum_out=o),
                waits=[(f"p0l{b}", p0_state["glv"][i])])
            p0_state["ghist"][i] = smark()

    def p0_thr():
        assert p0_state["g"] == NP0, "gamma pump incomplete"
        tot = AR.take([128, 4], F32)
        th11 = AR.take([128, 2], F32)
        c_gam = p0_state["ghist"][NP0 - 1]
        _ranges = {0: (0, 88), 1: (88, 176), 2: (176, 272)}
        for wi in range(3):
            cols = p0_wcols[wi]
            nelem = float(sum(p0_pieces[c][1] for c in cols) * 128)
            r0, r1 = _ranges[wi]
            vop(lambda h, c0=r0, c1=r1, o=tot[:, 0:1]:
                h.reduce_sum(o, p0_abs[:, c0:c1], axis=AX.X),
                waits=[("sc", c_gam)], inc=("vec", 1))
            mm(PS[7][0:1, 0:1], ones_fcol[:], tot[:, 0:1], True, True,
               waits=[("vec", pr.c("vec"))] + psw([7]))
            gid = mm_end()
            vop(lambda h, n=nelem: h.tensor_scalar(th11[0:1, 0:1], PS[7][0:1, 0:1],
                                                   0.5 / n, 0.5e-8, ALU.mult, ALU.add),
                waits=[("mm", gid)], inc=("vec", 1))
            ps_free[7] = ("vec", pr.c("vec"))
            mm(PS[7][:, 0:1], ones_frow[:], th11[0:1, 0:1], True, True,
               waits=[("vec", pr.c("vec"))] + psw([7]))
            gid = mm_end()
            vop(lambda h, o=thr_sb[:, 2 * wi:2 * wi + 1]: h.tensor_copy(o, PS[7][:, 0:1]),
                waits=[("mm", gid)])
            vop(lambda h, o=thr_sb[:, 2 * wi + 1:2 * wi + 2]:
                h.tensor_scalar(o, PS[7][:, 0:1], -1.0, None, ALU.mult))
            cv = vmark()
            ps_free[7] = ("vec", cv)
        p0_state["thr_done"] = pr.c("vec")

    def p0_load_quant(i):
        srcap, n, _, _ = p0_pieces[i]
        b = i % 2
        return dma("scalar", f"p0q{b}", p0_tile[b][:, 0:n], srcap,
                   waits=[("vec", p0_state["qhist"].get(i - 2, 0))])

    def p0_pump_quant(k):
        if p0_state["thr_done"] == 0:
            return
        geb, leb = p0_tile[2], p0_tile[3]
        for _ in range(k):
            i = p0_state["q"]
            if i >= NP0:
                return
            p0_state["q"] += 1
            if i == 0:
                for j in range(2):
                    p0_state["qlv"][j] = p0_load_quant(j)
            elif i + 1 < NP0:
                p0_state["qlv"][i + 1] = p0_load_quant(i + 1)
            srcap, n, dstap, wi = p0_pieces[i]
            b = i % 2
            w1st = [(f"p0q{b}", p0_state["qlv"][i]), ("vec", p0_state["thr_done"])]
            if i == 0:
                w1st.append(("sc", p0_state["ghist"][NP0 - 1]))
            vop(lambda h, t=p0_tile[b][:, 0:n], o=geb[:, 0:n],
                th=thr_sb[:, 2 * wi:2 * wi + 1]:
                h.tensor_scalar(o, t, th, None, ALU.is_ge),
                waits=w1st)
            vop(lambda h, t=p0_tile[b][:, 0:n], o=leb[:, 0:n],
                th=thr_sb[:, 2 * wi + 1:2 * wi + 2]:
                h.tensor_scalar(o, t, th, -1.0, ALU.is_le, ALU.mult))
            vspace()
            vop(lambda h, a=geb[:, 0:n], c=leb[:, 0:n], o=p0_wq[b][:, 0:n]:
                h.tensor_tensor(o, a, c, ALU.add),
                waits=[(f"ws{b}", pr.c(f"ws{b}"))])
            p0_state["qhist"][i] = vmark()
            dma("gpsimd", f"ws{b}", dstap, p0_wq[b][:, 0:n],
                waits=[("vec", p0_state["qhist"][i])])
            p0_state["wsrec"][i] = (pr.c("ws0"), pr.c("ws1"))

    def p0_quant_upto(piece):
        while p0_state["q"] <= piece:
            assert p0_state["thr_done"] > 0, "quant needed before thr ready"
            p0_pump_quant(8)
        return [("ws0", p0_state["wsrec"][piece][0]),
                ("ws1", p0_state["wsrec"][piece][1])]

    def p1a():
        a0 = P0SEG_END
        AR.reset(a0)
        TCH = 4
        xt = [AR.take([128, TCH, D], F32) for _ in range(2)]
        xnt = [AR.take([128, TCH, D], BF16) for _ in range(2)]
        st = AR.take([128, 8], F32)
        v_hist = {}
        for ti in range(NCHUNK // TCH):
            p0_pump_gamma(3)
            b = ti % 2
            lv = dma("sync", f"x{b}", xt[b][:], x_v[:, ti * TCH:(ti + 1) * TCH, :],
                     waits=[("vec", v_hist.get(ti - 2, 0))])
            for j in range(TCH):
                xc = xt[b][:, j, :]
                vop(lambda h, i=xc: h.reduce_sum(st[:, 0:1], i, axis=AX.X),
                    waits=[(f"x{b}", lv)])
                sop(lambda h, i=xc: h.activation(trash[:, 0:D], i, AF.Square,
                                                 accum_out=st[:, 1:2]),
                    waits=[(f"x{b}", lv), ("vec", pr.c("vec") + 0)], inc=("sc", 1))
                c_sq = pr.c("sc")
                vspace()
                vop(lambda h: h.tensor_scalar(st[:, 2:3], st[:, 0:1], 1.0 / D, None, ALU.mult))
                vop(lambda h: h.tensor_scalar(st[:, 3:4], st[:, 1:2], 1.0 / D, EPS_LN,
                                              ALU.mult, ALU.add),
                    waits=[("sc", c_sq)])
                vop(lambda h: h.tensor_mul(st[:, 4:5], st[:, 2:3], st[:, 2:3]))
                vspace()
                vop(lambda h: h.tensor_sub(st[:, 5:6], st[:, 3:4], st[:, 4:5]),
                    inc=("vec", 1))
                sop(lambda h: h.activation(st[:, 6:7], st[:, 5:6], AF.Sqrt),
                    waits=[("vec", pr.c("vec"))], inc=("sc", 1))
                vop(lambda h: h.reciprocal(st[:, 7:8], st[:, 6:7]),
                    waits=[("sc", pr.c("sc"))])
                vspace()
                vop(lambda h, i=xc, o=xnt[b][:, j, :]:
                    h.tensor_scalar(o, i, st[:, 2:3], st[:, 7:8], ALU.subtract, ALU.mult),
                    waits=[(f"xs{b}", pr.c(f"xs{b}"))],
                    inc=("vec", 1))
            v_hist[ti] = pr.c("vec")
            dma("gpsimd", f"xs{b}", xn_v[:, ti * TCH:(ti + 1) * TCH, :], xnt[b][:],
                waits=[("vec", v_hist[ti])])
        AR.reset(a0)

    def p1b():
        a0 = P0SEG_END
        AR.reset(a0)
        wq_sb = AR.take([128, DC, NH * HD], BF16)
        wk_sb = AR.take([128, DC, NKV * HD], BF16)
        wv_sb = AR.take([128, DC, NKV * HD], BF16)
        xnT = [AR.take([128, DC, 256], BF16)]
        q_tm = AR.take([128, NH, HD], F32)
        k_tm = AR.take([128, NKV, HD], F32)
        q_out = [AR.take([128, NH, HD], BF16) for _ in range(2)]
        k_out = [AR.take([128, NKV, HD], BF16) for _ in range(2)]
        v_bf = [AR.take([128, NKV * HD], BF16) for _ in range(2)]
        ropeb = [AR.take([128, 2 * ROT], F32) for _ in range(2)]
        ropebf = AR.take([128, 2 * ROT], BF16)
        tmp3 = AR.take([128, NH, ROT], BF16)
        prodA = AR.take([128, NH, ROT], BF16)
        ssq = AR.take([128, NH], F32)
        inv = AR.take([128, NH], F32)

        dma("gpsimd", "wres", wq_sb[:], wq_d.rearrange("(c p) f -> p c f", p=128))
        dma("gpsimd", "wres", wk_sb[:], wk_d.rearrange("(c p) f -> p c f", p=128))
        wres_v = dma("gpsimd", "wres", wv_sb[:], wv_d.rearrange("(c p) f -> p c f", p=128))

        def rms_rope(src3, nhead, out3, bcscale, ropetile, first_waits):
            src_all = src3[:, 0:nhead, :]
            srcR = src3[:, 0:nhead, 0:ROT]
            srcE = src3[:, 0:nhead, 0:ROT:2]
            srcO = src3[:, 0:nhead, 1:ROT:2]
            srcHI = src3[:, 0:nhead, ROT:HD]
            inv_bc = inv[:, 0:nhead].rearrange("p (h o) -> p h o", o=1).to_broadcast([128, nhead, HD])
            sc_bc = bcscale[:, 0:HD].rearrange("p (o f) -> p o f", o=1).to_broadcast([128, nhead, HD])
            cos_bc = ropetile[:, 0:ROT].rearrange("p (o f) -> p o f", o=1).to_broadcast([128, nhead, ROT])
            sinb_bc = ropebf[:, ROT:2 * ROT].rearrange("p (o f) -> p o f", o=1).to_broadcast([128, nhead, ROT])
            tmpN = tmp3[:, 0:nhead, :]
            tmpE = tmp3[:, 0:nhead, 0:ROT:2]
            tmpO = tmp3[:, 0:nhead, 1:ROT:2]
            prodN = prodA[:, 0:nhead, :]
            outR = out3[:, 0:nhead, 0:ROT]
            outHI = out3[:, 0:nhead, ROT:HD]
            for hh in range(nhead):
                sop(lambda h, i=src3[:, hh, :], o=ssq[:, hh:hh + 1]:
                    h.activation(trash[:, 0:HD], i, AF.Square, accum_out=o),
                    waits=first_waits if hh == 0 else ())
            c1 = smark()
            vop(lambda h, n=nhead: h.tensor_scalar(inv[:, 0:n], ssq[:, 0:n],
                                                   1.0 / HD, EPS_RMS, ALU.mult, ALU.add),
                waits=[("sc", c1)], inc=("vec", 1))
            sop(lambda h, n=nhead: h.activation(inv[:, 0:n], inv[:, 0:n], AF.Sqrt),
                waits=[("vec", pr.c("vec"))], inc=("sc", 1))
            vop(lambda h, n=nhead: h.reciprocal(inv[:, 0:n], inv[:, 0:n]),
                waits=[("sc", pr.c("sc"))])
            vspace()
            vop(lambda h, a=src_all, b=inv_bc: h.tensor_mul(a, a, b))
            vspace()
            vop(lambda h, a=src_all, b=sc_bc: h.tensor_mul(a, a, b))
            vspace()
            vop(lambda h, o=tmpE, i=srcO: h.tensor_copy(o, i))
            vop(lambda h, o=tmpO, i=srcE: h.tensor_copy(o, i))
            vop(lambda h, o=prodN, i=srcR, c=cos_bc: h.tensor_mul(o, i, c))
            vspace()
            vop(lambda h, a=tmpN, c=sinb_bc: h.tensor_mul(a, a, c))
            vop(lambda h, o=outHI, i=srcHI: h.tensor_copy(o, i))
            vop(lambda h, o=outR, a=prodN, b=tmpN: h.tensor_add(o, a, b))
            return vmark()

        mm_hist = {}
        rope_hist = {}
        for st_i in range(NCHUNK // 2):
            sb = 0
            xv = 0
            for dc in range(DC):
                xv = dma("sync", f"xt{sb}",
                         xnT[sb][:, dc, :],
                         xn_s[st_i * 256:(st_i + 1) * 256, dc * 128:(dc + 1) * 128],
                         waits=[("mm", mm_hist.get(st_i - 1, 0))] if dc == 0 else (),
                         transpose=True)
            for tj in range(2):
                c = st_i * 2 + tj
                p0_pump_gamma(6)
                rb = c % 2
                pb = c % 2
                rv = dma("sync", f"rp{rb}", ropeb[rb][:], rope_d[c],
                         waits=[("vec", rope_hist.get(c - 2, 0))])
                vop(lambda h, i=ropeb[rb][:, ROT:2 * ROT], o=ropebf[:, ROT:2 * ROT]:
                    h.tensor_copy(o, i), waits=[(f"rp{rb}", rv)])
                qw = psw([0, 1, 2, 3]) + [(f"xt{sb}", xv), ("wres", wres_v)]
                for dc in range(DC):
                    lhs = xnT[sb][:, dc, tj * 128:(tj + 1) * 128]
                    for blk in range(4):
                        mm(PS[blk][:], lhs, wq_sb[:, dc, blk * 512:(blk + 1) * 512],
                           dc == 0, dc == DC - 1,
                           waits=qw if (dc == 0 and blk == 0) else ())
                g_q = mm_end()
                kw_ = psw([4, 5])
                for dc in range(DC):
                    lhs = xnT[sb][:, dc, tj * 128:(tj + 1) * 128]
                    for blk in range(2):
                        mm(PS[4 + blk][:], lhs, wk_sb[:, dc, blk * 512:(blk + 1) * 512],
                           dc == 0, dc == DC - 1,
                           waits=kw_ if (dc == 0 and blk == 0) else ())
                g_k = mm_end()
                vw_ = psw([6, 7])
                for dc in range(DC):
                    lhs = xnT[sb][:, dc, tj * 128:(tj + 1) * 128]
                    for blk in range(2):
                        mm(PS[6 + blk][:], lhs, wv_sb[:, dc, blk * 512:(blk + 1) * 512],
                           dc == 0, dc == DC - 1,
                           waits=vw_ if (dc == 0 and blk == 0) else ())
                g_v = mm_end()
                q2 = q_tm.rearrange("p a b -> p (a b)")
                k2 = k_tm.rearrange("p a b -> p (a b)")
                for blk in range(4):
                    vop(lambda h, o=q2[:, blk * 512:(blk + 1) * 512], i=PS[blk][:]:
                        h.tensor_copy(o, i),
                        waits=[("mm", g_q)] if blk == 0 else ())
                cq = vmark()
                for blk in range(4):
                    ps_free[blk] = ("vec", cq)
                for blk in range(2):
                    vop(lambda h, o=k2[:, blk * 512:(blk + 1) * 512], i=PS[4 + blk][:]:
                        h.tensor_copy(o, i),
                        waits=[("mm", g_k)] if blk == 0 else ())
                ck = vmark()
                for blk in range(4, 6):
                    ps_free[blk] = ("vec", ck)
                for blk in range(2):
                    vop(lambda h, o=v_bf[pb][:, blk * 512:(blk + 1) * 512], i=PS[6 + blk][:]:
                        h.tensor_copy(o, i),
                        waits=[("mm", g_v), (f"vs{pb}", pr.c(f"vs{pb}"))] if blk == 0 else ())
                cv0 = vmark()
                for blk in range(6, 8):
                    ps_free[blk] = ("vec", cv0)
                dma("gpsimd", f"vs{pb}", v_v[:, c, :], v_bf[pb][:],
                    waits=[("vec", cv0)])
                pr.emit("vector", lambda h: h.nop(),
                        waits=[(f"qs{pb}", pr.c(f"qs{pb}")), (f"ks{pb}", pr.c(f"ks{pb}")),
                               (f"rp{rb}", rv)])
                rms_rope(q_tm, NH, q_out[pb], qs_bc, ropeb[rb], [("vec", cq)])
                cv = rms_rope(k_tm, NKV, k_out[pb], ks_bc, ropeb[rb], [("vec", pr.c("vec"))])
                rope_hist[c] = cv
                dma("gpsimd", f"qs{pb}", q_v[:, c, :],
                    q_out[pb].rearrange("p a b -> p (a b)"),
                    waits=[("vec", cv)])
                dma("gpsimd", f"ks{pb}", k_v[:, c, :],
                    k_out[pb].rearrange("p a b -> p (a b)"),
                    waits=[("vec", cv)])
            mm_hist[st_i] = pr.c("mm")

        AR.reset(a0)

    def p2():
        a0 = P0SEG_END
        AR.reset(a0)
        GC2 = 4
        NG2 = NCHUNK // GC2
        GQ2 = GC2 * 128
        wo_sb = AR.take([128, DC, D], BF16)
        kar = [AR.take([128, 2, 1536], BF16) for _ in range(2)]
        var_ = [AR.take([128, 12, HD], BF16) for _ in range(2)]
        qar = [AR.take([128, 2, GQ2], BF16) for _ in range(2)]
        PT = AR.take([128, 3, GQ2], BF16)
        tanh_sb = AR.take([128, GQ2], F32)
        linv = AR.take([128, GQ2], F32)
        rbc = AR.take([128, GQ2], F32)
        attn_sb = AR.take([128, DC, GQ2], BF16)
        s_st = [AR.take([128, D], F32) for _ in range(2)]

        wres_v = dma("gpsimd", "wres", wo_sb[:], wo_d.rearrange("(c p) f -> p c f", p=128))
        k_hist = {}
        q_hist = {}

        for g in range(NG2):
            c0 = g * GC2
            kbase = max(0, c0 - 8)
            nkc = c0 + 4 - kbase
            kcs = list(range(kbase, c0 + 4))
            for kvh in range(NKV):
                kb = kvh % 2
                kwaits = [("mm", k_hist.get(kb, 0))]
                kv_ = 0
                for hdc in range(2):
                    kv_ = dma("sync", f"ka{kb}", kar[kb][:, hdc, 0:nkc * 128],
                              k_s[kbase * 128:(c0 + 4) * 128,
                                  kvh * HD + hdc * 128:kvh * HD + (hdc + 1) * 128],
                              waits=kwaits if hdc == 0 else (), transpose=True)
                vv_ = dma("sync", f"va{kb}", var_[kb][:, 0:nkc, :],
                          v_v[:, kbase:c0 + 4, kvh * HD:(kvh + 1) * HD])
                for qh in range(2):
                    p0_pump_gamma(4)
                    head = kvh * 2 + qh
                    qb = head % 2
                    qv_ = 0
                    for hdc in range(2):
                        qv_ = dma("sync", f"qa{qb}", qar[qb][:, hdc, :],
                                  q_s[c0 * 128:(c0 + GC2) * 128,
                                      head * HD + hdc * 128:head * HD + (hdc + 1) * 128],
                                  waits=[("mm", q_hist.get(qb, 0))] if hdc == 0 else (),
                                  transpose=True)
                    atb = [3, 4] if qh == 0 else [5, 6]
                    l_first = True
                    at_first = [True, True]
                    last_kc = kcs[-1]
                    pvmm = {}
                    c_exp = {}
                    c_mask = {}
                    tanh_c = {}
                    SBANKS = [0, 1, 7]

                    def emit_s(kc):
                        qlo = max(kc, c0)
                        qhi = min(kc + 8, c0 + GC2 - 1)
                        ncols = (qhi - qlo + 1) * 128
                        qoff = (qlo - c0) * 128
                        kcol = (kc - kbase) * 128
                        sbk = SBANKS[kc % 3]
                        par = kc % 3
                        sw = psw([sbk]) + [(f"ka{kb}", kv_), (f"qa{qb}", qv_)]
                        for hdc in range(2):
                            mm(PS[sbk][:, 0:ncols],
                               kar[kb][:, hdc, kcol:kcol + 128],
                               qar[qb][:, hdc, qoff:qoff + ncols],
                               hdc == 0, hdc == 1,
                               waits=sw if hdc == 0 else ())
                        g_s = pr.mark_last("tensor", "mm")
                        sop(lambda h, o=tanh_sb[:, 0:ncols], i=PS[sbk][:, 0:ncols]:
                            h.activation(o, i, AF.Tanh, scale=1.0 / CAP),
                            waits=[("mm", g_s)])
                        tanh_c[kc] = smark()
                        ps_free[sbk] = ("sc", tanh_c[kc])
                        sop(lambda h, o=PT[:, par, 0:ncols], i=tanh_sb[:, 0:ncols]:
                            h.activation(o, i, AF.Exp, scale=CAP),
                            waits=[("mm", pvmm.get(kc - 3, 0))])
                        c_exp[kc] = smark()
                        nm = 0
                        if kc >= c0:
                            vop(lambda h, o=PT[:, par, 0:128]:
                                h.tensor_mul(o, o, mask_sb[:, 0:128]),
                                waits=[("sc", c_exp[kc])])
                            nm += 1
                        if kc + 8 <= c0 + GC2 - 1:
                            off = (kc + 8 - qlo) * 128
                            vop(lambda h, o=PT[:, par, off:off + 128]:
                                h.tensor_mul(o, o, mask_sb[:, 128:256]),
                                waits=[("sc", c_exp[kc])] if nm == 0 else ())
                            nm += 1
                        if nm > 0:
                            c_mask[kc] = (("vec", pr.mark_last("vector", "vec")),)
                        else:
                            c_mask[kc] = (("sc", c_exp[kc]),)

                    def emit_pv(kc):
                        nonlocal l_first
                        qlo = max(kc, c0)
                        qhi = min(kc + 8, c0 + GC2 - 1)
                        ncols = (qhi - qlo + 1) * 128
                        qoff = (qlo - c0) * 128
                        par = kc % 3
                        lw = list(c_mask[kc])
                        w_ = lw + (psw([2]) if l_first else [])
                        mm(PS[2][0:1, qoff:qoff + ncols],
                           ones_bcol[:], PT[:, par, 0:ncols],
                           l_first, kc == last_kc, waits=w_)
                        l_first = False
                        for hdc in range(2):
                            bi = atb[hdc]
                            w_ = lw + [(f"va{kb}", vv_)]
                            if at_first[hdc]:
                                w_ = w_ + psw([bi])
                            mm(PS[bi][:, qoff:qoff + ncols],
                               var_[kb][:, kc - kbase, hdc * 128:(hdc + 1) * 128],
                               PT[:, par, 0:ncols],
                               at_first[hdc], kc == last_kc, waits=w_)
                            at_first[hdc] = False
                        pvmm[kc] = pr.mark_last("tensor", "mm")

                    for idx, kc in enumerate(kcs):
                        emit_s(kc)
                        if idx >= 2:
                            emit_pv(kcs[idx - 2])
                    if len(kcs) >= 2:
                        emit_pv(kcs[-2])
                    emit_pv(last_kc)
                    g_all = pvmm[last_kc]
                    vop(lambda h: h.reciprocal(linv[0:1, 0:GQ2], PS[2][0:1, 0:GQ2]),
                        waits=[("mm", g_all)])
                    cv = vmark()
                    ps_free[2] = ("vec", cv)
                    mm(PS[7][:, 0:GQ2], ones_frow[:], linv[0:1, 0:GQ2], True, True,
                       waits=[("vec", cv)] + psw([7]))
                    g_bc = mm_end()
                    vop(lambda h: h.tensor_copy(rbc[:, 0:GQ2], PS[7][:, 0:GQ2]),
                        waits=[("mm", g_bc)])
                    cv = vmark()
                    ps_free[7] = ("vec", cv)
                    vspace()
                    for hdc in range(2):
                        bi = atb[hdc]
                        vop(lambda h, o=attn_sb[:, head * 2 + hdc, :],
                            i=PS[bi][:, 0:GQ2], r=rbc[:, 0:GQ2]:
                            h.tensor_mul(o, i, r))
                    cv = vmark()
                    for bi in atb:
                        ps_free[bi] = ("vec", cv)
                    q_hist[qb] = g_all
                k_hist[kb] = pr.c("mm")
            c_attn = pr.c("vec")
            for qt in range(GC2):
                c = c0 + qt
                par = qt % 2
                banks = [4 * par + i for i in range(4)]
                w0 = psw(banks) + [("vec", c_attn), ("wres", wres_v)]
                for hdc in range(DC):
                    lhs = attn_sb[:, hdc, qt * 128:(qt + 1) * 128]
                    for dcb in range(4):
                        mm(PS[banks[dcb]][:], lhs, wo_sb[:, hdc, dcb * 512:(dcb + 1) * 512],
                           hdc == 0, hdc == DC - 1,
                           waits=w0 if (hdc == 0 and dcb == 0) else ())
                g_wo = mm_end()
                for dcb in range(4):
                    vop(lambda h, o=s_st[par][:, dcb * 512:(dcb + 1) * 512], i=PS[banks[dcb]][:]:
                        h.tensor_copy(o, i),
                        waits=[("mm", g_wo), (f"ss{par}", pr.c(f"ss{par}"))] if dcb == 0 else ())
                cv = vmark()
                for b in banks:
                    ps_free[b] = ("vec", cv)
                dma("gpsimd", f"ss{par}", s_v[:, c, :], s_st[par][:],
                    waits=[("vec", cv)])
        while p0_state["g"] < NP0:
            p0_pump_gamma(4)
        p0_thr()
        AR.reset(a0)

    def p3():
        a0 = P0SEG_END
        AR.reset(a0)
        xnf = AR.take([128, DC, 2048], BF16)
        w1T = [AR.take([128, DC, 256], BF16) for _ in range(2)]
        w2T = [AR.take([128, DC, 256], BF16) for _ in range(2)]
        sig = AR.take([128, D], F32)
        hidst = [AR.take([128, D], BF16) for _ in range(2)]

        NS = HFF // 256
        mm_hist = {}
        c_tt_prev = 0
        it = 0
        for half in range(2):
            hb = half * 2048
            xv = 0
            for dc in range(DC):
                xv = dma("sync", "xnf", xnf[:, dc, :],
                         xn_s[hb * 1:hb + 2048, dc * 128:(dc + 1) * 128],
                         waits=[("mm", pr.c("mm"))] if dc == 0 else (),
                         transpose=True)
            xnf_done = xv
            for hs in range(NS):
                p0_pump_quant(10)
                sb = hs % 2
                wv1 = wv2 = 0
                wqdep = p0_quant_upto(8 * hs + 7)
                wqdep2 = wqdep
                for dc in range(DC):
                    wv1 = dma("sync", f"w1t{sb}", w1T[sb][:, dc, :],
                              w1q_s[hs * 256:(hs + 1) * 256, dc * 128:(dc + 1) * 128],
                              waits=(wqdep + [("mm", mm_hist.get((half, hs - 2), 0))])
                              if dc == 0 else (),
                              transpose=True)
                    wv2 = dma("sync", f"w2t{sb}", w2T[sb][:, dc, :],
                              w2q_s[hs * 256:(hs + 1) * 256, dc * 128:(dc + 1) * 128],
                              waits=wqdep2 if dc == 0 else (),
                              transpose=True)
                for hcl in range(2):
                    hc = hs * 2 + hcl
                    w_ = psw([0, 1, 2, 3]) + [("xnf", xnf_done), (f"w1t{sb}", wv1)]
                    for dc in range(DC):
                        lhs = w1T[sb][:, dc, hcl * 128:(hcl + 1) * 128]
                        for tb in range(4):
                            mm(PS[tb][:], lhs, xnf[:, dc, tb * 512:(tb + 1) * 512],
                               dc == 0, dc == DC - 1,
                               waits=w_ if (dc == 0 and tb == 0) else ())
                    g_1 = mm_end()
                    w_ = psw([4, 5, 6, 7]) + [(f"w2t{sb}", wv2)]
                    for dc in range(DC):
                        lhs = w2T[sb][:, dc, hcl * 128:(hcl + 1) * 128]
                        for tb in range(4):
                            mm(PS[4 + tb][:], lhs, xnf[:, dc, tb * 512:(tb + 1) * 512],
                               dc == 0, dc == DC - 1,
                               waits=w_ if (dc == 0 and tb == 0) else ())
                    g_2 = mm_end()
                    for tb in range(4):
                        sop(lambda h, o=sig[:, tb * 512:(tb + 1) * 512], i=PS[tb][:]:
                            h.activation(o, i, AF.Sigmoid),
                            waits=[("mm", g_1), ("vec", c_tt_prev)] if tb == 0 else ())
                    c_sg = smark()
                    for tb in range(4):
                        ps_free[tb] = ("sc", c_sg)
                    hp = it % 2
                    it += 1
                    for tb in range(4):
                        vop(lambda h, o=hidst[hp][:, tb * 512:(tb + 1) * 512],
                            i=PS[4 + tb][:], s_=sig[:, tb * 512:(tb + 1) * 512]:
                            h.tensor_mul(o, i, s_),
                            waits=[("mm", g_2), ("sc", c_sg),
                                   (f"hs{hp}", pr.c(f"hs{hp}"))] if tb == 0 else ())
                    c_tt_prev = vmark()
                    for tb in range(4, 8):
                        ps_free[tb] = ("vec", c_tt_prev)
                    dma("gpsimd", f"hs{hp}", hid_v[:, hc, hb:hb + 2048], hidst[hp][:],
                        waits=[("vec", c_tt_prev)])
                mm_hist[(half, hs)] = pr.c("mm")
        while p0_state["q"] < NP0:
            p0_pump_quant(8)
        AR.reset(a0)

    def p4():
        a0 = P0SEG_END
        AR.reset(a0)
        w3T = AR.take([128, HC, 1024], BF16)
        hid = [AR.take([128, HC, 256], BF16) for _ in range(2)]
        g_sb = AR.take([128, 1024], F32)
        s_t = [AR.take([128, 1024], F32) for _ in range(2)]
        x_t = [AR.take([128, 1024], F32) for _ in range(2)]
        o_t = [AR.take([128, 1024], F32) for _ in range(2)]

        c_tt_prev = 0
        for dh in range(2):
            w3_done = 0
            wqdep = p0_quant_upto(4 * HC + (8 * dh + 8) * 6 - 1)
            for hc in range(HC):
                w3_done = dma("sync", "w3t", w3T[:, hc, :],
                              w3q_s[dh * 1024:(dh + 1) * 1024, hc * 128:(hc + 1) * 128],
                              waits=(wqdep + [("mm", pr.c("mm"))]) if hc == 0 else (),
                              transpose=True)
            mm_hist = {}
            v_hist = {}
            for tb in range(16):
                sb = tb % 2
                hv = dma("sync", f"hl{sb}", hid[sb][:],
                         hid_v[:, :, tb * 256:(tb + 1) * 256],
                         waits=[("mm", mm_hist.get(tb - 2, 0))])
                for tcl in range(2):
                    c = tb * 2 + tcl
                    pb = c % 2
                    dma("sync", f"sx{pb}", s_t[pb][:],
                        s_v[:, c, dh * 1024:(dh + 1) * 1024],
                        waits=[("vec", v_hist.get(c - 2, 0))])
                    sxv = dma("sync", f"sx{pb}", x_t[pb][:],
                              x_v[:, c, dh * 1024:(dh + 1) * 1024])
                    banks = [2 * pb, 2 * pb + 1]
                    w_ = psw(banks) + [(f"hl{sb}", hv), ("w3t", w3_done)]
                    for hc in range(HC):
                        lhs = hid[sb][:, hc, tcl * 128:(tcl + 1) * 128]
                        for dcb in range(2):
                            mm(PS[banks[dcb]][:], lhs, w3T[:, hc, dcb * 512:(dcb + 1) * 512],
                               hc == 0, hc == HC - 1,
                               waits=w_ if (hc == 0 and dcb == 0) else ())
                    g_f = mm_end()
                    for dcb in range(2):
                        sop(lambda h, o=g_sb[:, dcb * 512:(dcb + 1) * 512], i=PS[banks[dcb]][:]:
                            h.activation(o, i, AF.Silu),
                            waits=[("mm", g_f), ("vec", c_tt_prev)] if dcb == 0 else ())
                    c_sl = smark()
                    for b in banks:
                        ps_free[b] = ("sc", c_sl)
                    vop(lambda h, o=o_t[pb], a=g_sb, s_=s_t[pb]:
                        h.tensor_mul(o, a, s_),
                        waits=[("sc", c_sl), (f"sx{pb}", sxv),
                               (f"os{pb}", pr.c(f"os{pb}"))])
                    vspace()
                    vop(lambda h, o=o_t[pb], b_=x_t[pb]: h.tensor_add(o, o, b_))
                    c_tt_prev = vmark()
                    v_hist[c] = c_tt_prev
                    dma("gpsimd", f"os{pb}", out_v[:, c, dh * 1024:(dh + 1) * 1024],
                        o_t[pb], waits=[("vec", c_tt_prev)])
                mm_hist[tb] = pr.c("mm")
        AR.reset(a0)

    if "setup" in phases or not phases:
        pass
    if "nosetup" not in phases:
        lvl = 4
        for p in phases:
            if p.startswith("lvl"):
                lvl = int(p[3:])
        setup(lvl)
    pr.barrier()
    if "p1a" in phases:
        p1a()
        pr.barrier()
    if "p1b" in phases:
        p1b()
        pr.barrier()
    if "p2" in phases:
        p2()
        pr.barrier()
    if "p3" in phases:
        p3()
        pr.barrier()
    if "p4" in phases:
        p4()
    pr.barrier()
    for stn in ("os0", "os1"):
        if pr.c(stn):
            pr.emit("gpsimd", lambda h: h.nop(), waits=[(stn, pr.c(stn))])

    with nc.Block() as block:
        @block.sync
        def _(e):
            pr.replay("sync", e)

        @block.gpsimd
        def _(e):
            pr.replay("gpsimd", e)

        @block.tensor
        def _(e):
            pr.replay("tensor", e)

        @block.vector
        def _(e):
            pr.replay("vector", e)

        @block.scalar
        def _(e):
            pr.replay("scalar", e)

    return nc


def host_consts():
    theta = (1.0 / 10000.0 ** (np.arange(0, ROT, 2, dtype=np.float32) / ROT)).astype(np.float32)
    pos = np.arange(S, dtype=np.float32)
    ang = pos[:, None] * theta[None, :]
    cos2 = np.repeat(np.cos(ang).astype(np.float32), 2, axis=1)
    sin2 = np.repeat(np.sin(ang).astype(np.float32), 2, axis=1)
    sin2[:, 0::2] *= -1.0
    tab = np.concatenate([cos2, sin2], axis=1).reshape(NCHUNK, 128, 2 * ROT)
    k = np.arange(128)[:, None]
    q = np.arange(128)[None, :]
    diag = (q >= k).astype(ml_dtypes.bfloat16)
    far = (q < k).astype(ml_dtypes.bfloat16)
    masks = np.concatenate([diag, far], axis=1)
    return np.ascontiguousarray(tab), np.ascontiguousarray(masks)


_NC_CACHE = {}


def _get_nc(dump=False, phases=None):
    key = (dump, phases)
    if key not in _NC_CACHE:
        kw = {} if phases is None else {"phases": phases}
        _NC_CACHE[key] = build_nc(dump=dump, **kw)
    return _NC_CACHE[key]


def kernel(L0_attn_x, L0_attn_Wq, L0_attn_Wk, L0_attn_Wv, L0_attn_Wo,
           L0_attn_q_scale, L0_attn_k_scale, L0_ffn_w1, L0_ffn_w2, L0_ffn_w3,
           _dump=False, _phases=None):
    tab, masks = host_consts()
    x = np.ascontiguousarray(np.asarray(L0_attn_x, dtype=np.float32))
    base = {
        "wq": np.ascontiguousarray(np.asarray(L0_attn_Wq, np.float32)),
        "wk": np.ascontiguousarray(np.asarray(L0_attn_Wk, np.float32)),
        "wv": np.ascontiguousarray(np.asarray(L0_attn_Wv, np.float32)),
        "wo": np.ascontiguousarray(np.asarray(L0_attn_Wo, np.float32)),
        "qsc": np.ascontiguousarray(np.asarray(L0_attn_q_scale, np.float32).reshape(1, HD)),
        "ksc": np.ascontiguousarray(np.asarray(L0_attn_k_scale, np.float32).reshape(1, HD)),
        "w1": np.ascontiguousarray(np.asarray(L0_ffn_w1, np.float32)),
        "w2": np.ascontiguousarray(np.asarray(L0_ffn_w2, np.float32)),
        "w3": np.ascontiguousarray(np.asarray(L0_ffn_w3, np.float32)),
        "ropetab": tab,
        "maskbf": masks,
    }
    in_maps = [dict(base, x=np.ascontiguousarray(x[c])) for c in range(8)]
    nc = _get_nc(dump=_dump, phases=None if _phases is None else tuple(_phases))
    res = run_bass_kernel_spmd(nc, in_maps, core_ids=list(range(8)))
    if _dump:
        return res
    return np.stack([r["out"] for r in res.results], axis=0)
